# version 01
# baseline (speedup 1.0000x reference)
import contextlib
import numpy as np
import concourse.bass as bass
import concourse.mybir as mybir
from concourse.bass_utils import run_bass_kernel_spmd

F32 = mybir.dt.float32
BF16 = mybir.dt.bfloat16
AF = mybir.ActivationFunctionType
ALU = mybir.AluOpType
AXX = mybir.AxisListType.X

T = 2048
HT = 1024
HALO = 3
D = 4096
KT = D // 128
INTER = 8192
NG = 8
NH = 128
HD = 64
GH = 16
NCH = 16
PROJ = 18560
EPS = 1e-5
DT_MIN, DT_MAX = 1e-3, 100.0
SEG_CLAMP = 30.0

ENGINES = ("sync", "scalar", "tensor", "vector", "gpsimd")

N_MT = 145
M_B0, M_DT, M_C0, M_G0 = 64, 72, 73, 81


def mt_kind(m):
    if m < M_B0:
        return "x"
    if m < M_DT:
        return "B"
    if m == M_DT:
        return "dt"
    if m < M_G0:
        return "C"
    return "gate"


class Plan:
    def __init__(self):
        self.ops = {e: [] for e in ENGINES}
        self.cnt = {}
        self.handles = {}

    def emit(self, eng, fn):
        self.ops[eng].append(fn)

    def wait(self, eng, sem, val):
        if val is None or val <= 0:
            return
        self.emit(eng, lambda E, s=sem, v=val: E.wait_ge(self.handles[s], v))

    fence_ops = {}

    def step(self, eng, fn, inc=None, n=1):
        if inc is None:
            self.emit(eng, fn)
            return None
        self.cnt[inc] = self.cnt.get(inc, 0) + n
        v = self.cnt[inc]
        fence = self.fence_ops.get(eng) if inc.startswith("c_") else None
        if fence is not None:
            self.emit(eng, fn)

            def run(E, f=fence, s=inc, k=n):
                f(E).then_inc(self.handles[s], k)
        else:
            def run(E, f=fn, s=inc, k=n):
                f(E).then_inc(self.handles[s], k)

        self.emit(eng, run)
        return v

    def val(self, sem):
        return self.cnt.get(sem, 0)

    def phase_barrier(self):
        vals = [(sem, v) for sem, v in self.cnt.items() if v > 0]
        for eng in ENGINES:
            for sem, v in vals:
                self.wait(eng, sem, v)

    def wait_all_dma(self, eng):
        for sem, v in list(self.cnt.items()):
            if sem.startswith("d_") and v > 0:
                self.wait(eng, sem, v)

    def step_fenced(self, eng, fn, inc, fence_fn):
        self.emit(eng, fn)
        return self.step(eng, fence_fn, inc=inc)


SEMS = ("d_sp", "d_act", "d_pool", "c_pe", "c_dve", "c_act", "c_pool", "c_cc")


def build_program(debug=(), phases="AMBCD", b_limit=10 ** 9):
    nc = bass.Bass("TRN2", target_bir_lowering=False)

    def dbg(name):
        return "ExternalOutput" if name in debug else "Internal"

    i_xT = nc.dram_tensor("xT", [D, T + HALO], F32, kind="ExternalInput").ap()
    i_xres = nc.dram_tensor("xres", [T, D], F32, kind="ExternalInput").ap()
    i_W = nc.dram_tensor("Wt", [N_MT, 128, KT, 128], F32, kind="ExternalInput").ap()
    i_cw = nc.dram_tensor("cwT", [128, 80 * 4], F32, kind="ExternalInput").ap()
    i_cb = nc.dram_tensor("cbT", [128, 80], F32, kind="ExternalInput").ap()
    i_dtb = nc.dram_tensor("dtb", [128, 1], F32, kind="ExternalInput").ap()
    i_Dt = nc.dram_tensor("Dtile", [128, 128], F32, kind="ExternalInput").ap()
    i_nw = nc.dram_tensor("normw", [128, INTER], F32, kind="ExternalInput").ap()
    i_wop = nc.dram_tensor("WopT", [16, 128, 64, 256], F32, kind="ExternalInput").ap()
    i_wffn = nc.dram_tensor("WffnT", [16, 128, 32, 256], F32, kind="ExternalInput").ap()
    i_acat = nc.dram_tensor("AcatT", [128, KT, 128], F32, kind="ExternalInput").ap()
    i_bflat = nc.dram_tensor("Bflat", [128, D], F32, kind="ExternalInput").ap()
    i_bias = nc.dram_tensor("biasrow", [1, D], F32, kind="ExternalInput").ap()
    i_lw = nc.dram_tensor("lwtile", [128, 128], F32, kind="ExternalInput").ap()
    i_I = nc.dram_tensor("I128", [128, 128], F32, kind="ExternalInput").ap()
    i_triu = nc.dram_tensor("triu01", [128, 128], F32, kind="ExternalInput").ap()
    i_A = nc.dram_tensor("Acol", [128, 1], F32, kind="ExternalInput").ap()
    i_w0 = nc.dram_tensor("w0mask", [128, 1], F32, kind="ExternalInput").ap()
    i_cv = nc.dram_tensor("cvec", [128, 4], F32, kind="ExternalInput").ap()

    o_out = nc.dram_tensor("out", [T, D], F32, kind="ExternalOutput").ap()

    s_gate = nc.dram_tensor("gate_s", [INTER, T], BF16, kind=dbg("gate_s"))
    s_xc = nc.dram_tensor("xconv_s", [INTER, T], BF16, kind=dbg("xconv_s"))
    s_bt = nc.dram_tensor("bt_s", [NG * 128, T], BF16, kind=dbg("bt_s"))
    s_ct = nc.dram_tensor("ct_s", [NG * 128, T], BF16, kind=dbg("ct_s"))
    s_btok = nc.dram_tensor("btok_s", [T, NG * 128], BF16, kind=dbg("btok_s"))
    s_csr = nc.dram_tensor("csr_s", [NCH, NH, 128], F32, kind=dbg("csr_s"))
    s_cslast = nc.dram_tensor("cslast_s", [NCH, 128], F32, kind=dbg("cslast_s"))
    s_mc = nc.dram_tensor("mc_s", [128, NG * 1024], F32, kind="Internal")
    s_gath = nc.dram_tensor("gath_s", [256, NG * 1024], F32, kind="Internal")
    s_y = nc.dram_tensor("y_s", [T, INTER], BF16, kind=dbg("y_s"))
    s_x2 = nc.dram_tensor("x2_s", [T, D], BF16, kind=dbg("x2_s"))
    s_wopb = nc.dram_tensor("wopb_s", [16, 128, 64, 256], BF16, kind="Internal")
    s_dbg1 = nc.dram_tensor("dbg1", [128, 8192], F32, kind=dbg("dbg1"))
    s_dbg2 = nc.dram_tensor("dbg2", [128, 2048], F32, kind=dbg("dbg2"))
    s_dbg3 = nc.dram_tensor("dbg3", [128, 2048], BF16, kind=dbg("dbg3"))
    s_dbg4 = nc.dram_tensor("dbg4", [128, 128], BF16, kind=dbg("dbg4"))
    s_dbg5 = nc.dram_tensor("dbg5", [128, 1024], BF16, kind=dbg("dbg5"))
    s_dbg6 = nc.dram_tensor("dbg6", [128, 1024], F32, kind=dbg("dbg6"))
    s_dbg7 = nc.dram_tensor("dbg7", [128, 2048], F32, kind=dbg("dbg7"))
    s_dbg8 = nc.dram_tensor("dbg8", [128, 1024], F32, kind=dbg("dbg8"))
    s_dbg9 = nc.dram_tensor("dbg9", [128, 1024], F32, kind=dbg("dbg9"))
    s_dbgA = nc.dram_tensor("dbgA", [128, 4], F32, kind=dbg("dbgA"))
    s_dbgB = nc.dram_tensor("dbgB", [128, 8192], BF16, kind=dbg("dbgB"))
    s_dbgC = nc.dram_tensor("dbgC", [128, NCH * 128], BF16, kind=dbg("dbgC"))
    s_dbgD = nc.dram_tensor("dbgD", [128, 128], BF16, kind=dbg("dbgD"))
    s_dbgE = nc.dram_tensor("dbgE", [128, 1024], BF16, kind=dbg("dbgE"))
    s_dbgF = nc.dram_tensor("dbgF", [128, 2048], BF16, kind=dbg("dbgF"))
    s_dbgG = nc.dram_tensor("dbgG", [128, 1024], F32, kind=dbg("dbgG"))
    s_dbgH = nc.dram_tensor("dbgH", [128, 1024], F32, kind=dbg("dbgH"))
    s_dbgI = nc.dram_tensor("dbgI", [128, NG * 1024], F32, kind=dbg("dbgI"))
    s_dbgJ = nc.dram_tensor("dbgJ", [256, NG * 1024], F32, kind=dbg("dbgJ"))
    s_wffb = nc.dram_tensor("wffb_s", [16, 128, 32, 256], BF16, kind="Internal")

    P = Plan()

    big = contextlib.ExitStack()
    with big:
        def SB(st, name, shape, dt=F32):
            return st.enter_context(nc.sbuf_tensor(name, shape, dt))

        def PS(st, name, shape, dt=F32):
            return st.enter_context(nc.psum_tensor(name, shape, dt))

        k_cw = SB(big, "k_cw", [128, 320])
        k_cb = SB(big, "k_cb", [128, 80])
        k_dtb = SB(big, "k_dtb", [128, 1])
        k_D = SB(big, "k_D", [128, 128])
        k_I16 = SB(big, "k_I16", [128, 128], BF16)
        k_I32 = SB(big, "k_I32", [128, 128])
        k_tri = SB(big, "k_tri", [128, 128])
        k_A = SB(big, "k_A", [128, 1])
        k_w0 = SB(big, "k_w0", [128, 1])
        k_ones = SB(big, "k_ones", [1, 128], BF16)
        k_cv = SB(big, "k_cv", [128, 4])
        k_fence = SB(big, "k_fence", [1, 4])
        FENCE = lambda E: E.tensor_copy(k_fence[0:1, 0:2], k_fence[0:1, 2:4])
        P.fence_ops = {
            "vector": FENCE,
            "scalar": lambda E: E.copy(k_fence[0:1, 0:1], k_fence[0:1, 1:2]),
        }
        stAB = contextlib.ExitStack()
        r_csT = SB(stAB, "r_csT", [128, T])
        r_dtokb = SB(stAB, "r_dtokb", [128, NCH, 128], BF16)
        ps_msc = PS(big, "ps_msc", [128, 256])
        ps_tr32 = ps_msc.ap()[:, 0:128]
        ps_tr = ps_msc.ap().bitcast(BF16)[:, 256:384]

        for dst, src in ((k_cw, i_cw), (k_cb, i_cb), (k_dtb, i_dtb), (k_D, i_Dt),
                         (k_I32, i_I), (k_tri, i_triu), (k_A, i_A), (k_w0, i_w0),
                         (k_cv, i_cv)):
            P.step("sync", lambda E, d=dst, s=src: E.dma_start(out=d[:], in_=s),
                   inc="d_cst", n=16)
        d_const = P.val("d_cst")
        P.wait("vector", "d_cst", d_const)
        P.wait("scalar", "d_cst", d_const)
        P.step("vector", lambda E: E.tensor_copy(k_I16[:], k_I32[:]), inc="c_dve")
        P.step("vector", lambda E: E.memset(k_ones[:], 1.0), inc="c_dve")

        stA = contextlib.ExitStack()
        with stA:
            a_XT = SB(stA, "a_XT", [128, KT, HT + HALO], BF16)
            a_w = [SB(stA, f"a_w{i}", [128, KT, 128], BF16) for i in range(2)]
            a_row = [SB(stA, f"a_row{i}", [128, HT + HALO]) for i in range(2)]
            a_brow = [SB(stA, f"a_brow{i}", [128, HT], BF16) for i in range(2)]
            a_t1 = [SB(stA, f"a_t1{i}", [128, HT]) for i in range(2)]
            a_t2 = SB(stA, "a_t2", [128, HT])
            a_tb = SB(stA, "a_tb", [128, HT], BF16)
            a_dtT = SB(stA, "a_dtT", [128, T])
            a_wb = [SB(stA, f"a_wb{i}", [128, 4096], BF16) for i in range(2)]
            a_csl = SB(stA, "a_csl", [16, 128])
            m_x = [SB(stA, f"m_x{i}", [128, 1024], BF16) for i in range(3)]
            m_bk = [SB(stA, f"m_bk{i}", [128, NG * 128], BF16) for i in range(2)]
            m_c = [SB(stA, f"m_c{i}", [128, 1024]) for i in range(2)]
            m_cst = SB(stA, "m_cst", [128, 128])
            m_bc = SB(stA, "m_bc", [128, 128])
            m_ebc = SB(stA, "m_ebc", [128, 128])
            m_dcy = SB(stA, "m_dcy", [128, 128])
            m_ddb = SB(stA, "m_ddb", [128, 128], BF16)
            m_xdd = SB(stA, "m_xdd", [128, 1024], BF16)
            m_t = SB(stA, "m_t", [128, 1024])

            ps_A = PS(stA, "ps_A", [128, 512])
            ps_B = PS(stA, "ps_B", [128, 512])
            ps_hl = PS(stA, "ps_hl", [128, 128])
            ps_st = PS(stA, "ps_st", [128, 1024])

            d_xt = 0
            for q in range(4):
                d_xt = P.step("gpsimd", lambda E, q=q: E.dma_start(
                    out=a_XT[:, q * 8:(q + 1) * 8, :],
                    in_=i_xT.rearrange("(k p) t -> p k t", p=128)
                    [:, q * 8:(q + 1) * 8, 0:HT + HALO]), inc="d_xt", n=16)

            precast = [("wop", w, q) for w in range(16) for q in range(4)] + \
                      [("wffn", w, q) for w in range(16) for q in range(2)]
            pc_i = 0
            d_wcast = 0
            wb_free = [0, 0]

            def emit_precast_one():
                nonlocal pc_i, d_wcast
                if pc_i >= len(precast):
                    return
                kind, w, q = precast[pc_i]
                par = pc_i % 2
                pc_i += 1
                if kind == "wop":
                    src = i_wop[w].rearrange("p a b -> p (a b)")[:, q * 4096:(q + 1) * 4096]
                    dst = s_wopb[w].rearrange("p a b -> p (a b)")[:, q * 4096:(q + 1) * 4096]
                else:
                    src = i_wffn[w].rearrange("p a b -> p (a b)")[:, q * 4096:(q + 1) * 4096]
                    dst = s_wffb[w].rearrange("p a b -> p (a b)")[:, q * 4096:(q + 1) * 4096]
                P.wait("gpsimd", f"d_wbs{par}", wb_free[par])
                v = P.step("gpsimd", lambda E, s=src, p=par: E.dma_start(
                    out=a_wb[p][:], in_=s), inc=f"d_wbp{par}", n=16)
                P.wait("sync", f"d_wbp{par}", v)
                d_wcast = wb_free[par] = P.step("sync", lambda E, d=dst, p=par: E.dma_start(
                    out=d, in_=a_wb[p][:]), inc=f"d_wbs{par}", n=16)

            dt_done = {"sp": 0, "dve": 0}
            mc_store = [0] * NG
            mini_emitted = 0

            def emit_mini_chunk(c):
                P.wait_all_dma("scalar")
                ldx = [0] * NG
                P.wait("scalar", "c_pe", mini_war_bk[c % 2])
                ldbk = P.step("scalar", lambda E, c=c: E.dma_start(
                    out=m_bk[c % 2][:], in_=s_btok.ap()[c * 128:(c + 1) * 128, :]),
                    inc=f"d_mbk{c % 2}", n=16)
                P.wait("sync", "d_csl", dt_done["sp"])
                P.wait("sync", "c_dve", P.val("c_dve"))
                P.wait("sync", "c_act", P.val("c_act"))
                ldbc = P.step("sync", lambda E, c=c: E.dma_start(
                    out=m_bc[:], in_=s_cslast.ap()[c:c + 1, :].broadcast_to([128, 128])),
                    inc="d_mbc", n=16)
                P.wait("tensor", "c_dve", P.val("c_dve"))
                P.step("tensor", lambda E, c=c: E.transpose(
                    ps_tr32[:], r_csT[:, c * 128:(c + 1) * 128], k_I32[:]), inc="c_pe")
                pv = P.val("c_pe")
                P.wait("vector", "c_pe", pv)
                P.step("vector", lambda E: E.tensor_copy(m_cst[:], ps_tr32[:]), inc="c_dve")
                P.wait("vector", "d_mbc", ldbc)
                P.step("vector", lambda E: E.tensor_sub(m_dcy[:], m_bc[:], m_cst[:]),
                       inc="c_dve")
                cv = P.val("c_dve")
                P.wait("scalar", "c_dve", cv)
                P.step("scalar", lambda E: E.activation(out=m_dcy[:], in_=m_dcy[:],
                                                        func=AF.Exp), inc="c_act")
                ca = P.step("scalar", lambda E: E.activation(out=m_ebc[:], in_=m_bc[:],
                                                             func=AF.Exp), inc="c_act")
                P.wait("vector", "c_act", ca)
                ddv = P.step("vector", lambda E, c=c: E.tensor_tensor(
                    m_ddb[:], r_dtokb[:, c, :], m_dcy[:], ALU.mult), inc="c_dve")
                for g in range(NG):
                    i = (c * NG + g) % 3
                    P.wait("scalar", "c_dve", mini_war_x[i])
                    ldx[g] = P.step("scalar", lambda E, c=c, g=g, i=i:
                                    E.dma_start_transpose(
                                        m_x[i][:],
                                        s_xc.ap()[g * 1024:(g + 1) * 1024,
                                                  c * 128:(c + 1) * 128]),
                                    inc=f"d_mx{i}", n=16)
                    P.wait("vector", f"d_mx{i}", ldx[g])
                    P.wait("vector", "c_pe", mini_war_xdd[0])
                    xddv = P.step_fenced("vector", lambda E, g=g, i=i: E.tensor_tensor(
                        m_xdd.ap().rearrange("p (h e) -> p h e", e=HD),
                        m_x[i].ap().rearrange("p (h e) -> p h e", e=HD),
                        m_ddb[:, g * GH:(g + 1) * GH, None].broadcast_to([128, GH, HD]),
                        ALU.mult), "c_dve", FENCE)
                    mini_war_x[i] = xddv
                    P.wait("tensor", "c_dve", xddv)
                    P.wait("tensor", f"d_mbk{c % 2}", ldbk)
                    P.wait("tensor", "c_dve", mini_war_st[0])

                    def mmm(E, g=g, c=c):
                        for h in range(GH):
                            ins = E.matmul(ps_st[:, h * HD:(h + 1) * HD],
                                           m_bk[c % 2][:, g * 128:(g + 1) * 128],
                                           m_xdd[:, h * HD:(h + 1) * HD],
                                           start=True, stop=True)
                        return ins
                    pv = P.step("tensor", mmm, inc="c_pe")
                    mini_war_xdd[0] = pv
                    mini_war_bk[c % 2] = pv
                    i2 = g % 2
                    if c > 0:
                        P.wait("sync", f"d_mcs{g}", mc_store[g])
                        wg, wv2 = mini_war_mc2[i2]
                        P.wait("sync", f"d_mcs{wg}", wv2)
                        ld = P.step("sync", lambda E, g=g, i2=i2: E.dma_start(
                            out=m_c[i2][:], in_=s_mc.ap()[:, g * 1024:(g + 1) * 1024]),
                            inc=f"d_mc{i2}", n=16)
                        P.wait("vector", f"d_mc{i2}", ld)
                        P.wait("vector", "c_act", ca)
                        P.step("vector", lambda E, g=g, i2=i2: E.tensor_tensor(
                            m_t.ap().rearrange("p (h e) -> p h e", e=HD),
                            m_c[i2].ap().rearrange("p (h e) -> p h e", e=HD),
                            m_ebc[:, g * GH:(g + 1) * GH, None].broadcast_to([128, GH, HD]),
                            ALU.mult), inc="c_dve")
                        P.wait("vector", "c_pe", pv)
                        uv = P.step("vector", lambda E, i2=i2: E.tensor_add(
                            m_c[i2][:], m_t[:], ps_st[:]), inc="c_dve")
                    else:
                        P.wait("vector", "c_pe", pv)
                        wg, wv2 = mini_war_mc2[i2]
                        P.wait("vector", f"d_mcs{wg}", wv2)
                        uv = P.step("vector", lambda E, i2=i2: E.tensor_copy(
                            m_c[i2][:], ps_st[:]), inc="c_dve")
                    mini_war_st[0] = uv
                    P.wait("sync", "c_dve", uv)
                    mc_store[g] = P.step("sync", lambda E, g=g, i2=i2: E.dma_start(
                        out=s_mc.ap()[:, g * 1024:(g + 1) * 1024], in_=m_c[i2][:]),
                        inc=f"d_mcs{g}", n=16)
                    mini_war_mc2[i2] = (g, mc_store[g])

            mini_war_x = [0, 0, 0]
            mini_war_bk = [0, 0]
            mini_war_st = [0]
            mini_war_xdd = [0]
            mini_war_mc = [0, 0]
            mini_war_mc2 = [(0, 0), (0, 0)]

            wload = {}
            t1_free = [0, 0]
            t2_free = 0
            row_free = [0, 0]
            brow_free = [0, 0]
            psA_free = 0
            psB_free = 0
            pstr_free = 0
            pe_gdone = {}

            def w_dma(gi):
                par = gi % 2
                if gi - 2 in pe_gdone:
                    P.wait("gpsimd", "c_pe", pe_gdone[gi - 2])
                wload[gi] = P.step("gpsimd", lambda E, m=gi % N_MT, p=par: E.dma_start(
                    out=a_w[p][:], in_=i_W[m]), inc=f"d_w{par}", n=16)

            mini_sched = {}
            for c in range(NCH):
                mini_sched.setdefault(74 + c * 4, []).append(c)

            w_dma(0)
            w_dma(1)
            for h in range(2):
                if h == 1:
                    P.wait("gpsimd", "c_pe", pe_gdone[N_MT - 1])
                    for q in range(4):
                        d_xt = P.step("gpsimd", lambda E, q=q: E.dma_start(
                            out=a_XT[:, q * 8:(q + 1) * 8, :],
                            in_=i_xT.rearrange("(k p) t -> p k t", p=128)
                            [:, q * 8:(q + 1) * 8, HT:T + HALO]), inc="d_xt", n=16)
                for m in range(N_MT):
                    gi = h * N_MT + m
                    kind = mt_kind(m)
                    par = gi % 2
                    P.wait("tensor", f"d_w{par}", wload[gi])
                    P.wait("tensor", "d_xt", d_xt)
                    P.wait("tensor", "c_dve", psA_free)
                    has_halo = kind in ("x", "B", "C")

                    def mmA(E, m=m, p=par):
                        for kt in range(KT):
                            ins = E.matmul(ps_A[:], a_w[p][:, kt, :],
                                           a_XT[:, kt, 3:515],
                                           start=(kt == 0), stop=(kt == KT - 1))
                        return ins
                    peA = P.step("tensor", mmA, inc="c_pe")
                    P.wait("tensor", "c_dve", psB_free)

                    def mmB(E, m=m, p=par, halo=has_halo):
                        for kt in range(KT):
                            first, last = kt == 0, kt == KT - 1
                            ins = E.matmul(ps_B[:], a_w[p][:, kt, :],
                                           a_XT[:, kt, 515:1027],
                                           start=first, stop=last)
                            if halo:
                                ins = E.matmul(ps_hl[:, 0:3], a_w[p][:, kt, :],
                                               a_XT[:, kt, 0:3], start=first, stop=last)
                        return ins
                    peB = P.step("tensor", mmB, inc="c_pe")
                    pe_gdone[gi] = peB
                    if gi + 2 < 2 * N_MT:
                        w_dma(gi + 2)
                    if m % 2 == 0:
                        emit_precast_one()

                    tok0 = h * HT

                    if kind == "gate":
                        g = m - M_G0
                        P.wait("vector", "c_pe", peA)
                        P.wait("vector", f"d_br{par}", brow_free[par])
                        P.step("vector", lambda E, p=par: E.tensor_copy(
                            a_brow[p][:, 0:512], ps_A[:]), inc="c_dve")
                        psA_free = P.val("c_dve")
                        P.wait("vector", "c_pe", peB)
                        ev = P.step("vector", lambda E, p=par: E.tensor_copy(
                            a_brow[p][:, 512:1024], ps_B[:]), inc="c_dve")
                        psB_free = ev
                        P.wait("sync", "c_dve", ev)
                        brow_free[par] = P.step("sync", lambda E, g=g, p=par, t0=tok0:
                                                E.dma_start(
                                                    out=s_gate.ap()[g * 128:(g + 1) * 128,
                                                                    t0:t0 + HT],
                                                    in_=a_brow[p][:]), inc="d_br" + str(par), n=16)
                        for mm_ in (mini_sched.get(m, []) if (h == 1 and "M" in phases) else []):
                            emit_mini_chunk(mm_)
                        continue

                    P.wait("vector", "c_pe", peA)
                    P.wait("vector", "c_dve", row_free[par])
                    P.step("vector", lambda E, p=par: E.tensor_copy(
                        a_row[p][:, 3:515], ps_A[:]), inc="c_dve")
                    psA_free = P.val("c_dve")
                    P.wait("vector", "c_pe", peB)
                    P.step("vector", lambda E, p=par: E.tensor_copy(
                        a_row[p][:, 515:1027], ps_B[:]), inc="c_dve")
                    if has_halo:
                        P.step("vector", lambda E, p=par: E.tensor_copy(
                            a_row[p][:, 0:3], ps_hl[:, 0:3]), inc="c_dve")
                    psB_free = P.val("c_dve")

                    if kind == "dt":
                        P.wait("vector", "c_act", t1_free[par])
                        P.wait("vector", "d_btk", t2_free)
                        P.step("vector", lambda E, p=par: E.tensor_scalar_add(
                            a_t1[p][:], a_row[p][:, 3:1027], k_dtb[:]), inc="c_dve")
                        row_free[par] = P.val("c_dve")
                        cv = P.val("c_dve")
                        P.wait("scalar", "c_dve", cv)
                        P.step("scalar", lambda E, p=par: E.activation(
                            out=a_t2[:], in_=a_t1[p][:], func=AF.Abs), inc="c_act")
                        P.step("scalar", lambda E: E.activation(
                            out=a_t2[:], in_=a_t2[:], func=AF.Exp, scale=-1.0),
                            inc="c_act")
                        ca = P.step("scalar", lambda E: E.activation(
                            out=a_t2[:], in_=a_t2[:], func=AF.Ln, bias=1.0), inc="c_act")
                        P.wait("vector", "c_act", ca)
                        P.step("vector", lambda E, p=par: E.tensor_single_scalar(
                            a_t1[p][:], a_t1[p][:], 0.0, ALU.max), inc="c_dve")
                        P.step("vector", lambda E, p=par, t0=tok0: E.tensor_add(
                            a_dtT[:, t0:t0 + HT], a_t1[p][:], a_t2[:]), inc="c_dve")
                        P.step("vector", lambda E, t0=tok0: E.tensor_scalar(
                            a_dtT[:, t0:t0 + HT], a_dtT[:, t0:t0 + HT],
                            DT_MIN, DT_MAX, ALU.max, ALU.min), inc="c_dve")
                        P.step("vector", lambda E, p=par, t0=tok0: E.tensor_scalar_mul(
                            a_t1[p][:], a_dtT[:, t0:t0 + HT], k_A[:]), inc="c_dve")
                        for cc in range(8):
                            P.step("vector", lambda E, p=par, cc=cc, t0=tok0:
                                   E.tensor_tensor_scan(
                                       r_csT[:, t0 + cc * 128:t0 + (cc + 1) * 128],
                                       a_t1[p][:, cc * 128:(cc + 1) * 128],
                                       a_t2[:, 0:128], 0.0, ALU.add, ALU.bypass),
                                   inc="c_dve")
                        t1_free[par] = 0
                        P.wait("vector", "d_btk", t2_free)
                        P.step_fenced("vector", lambda E, t0=tok0: E.tensor_copy(
                            a_tb[:], a_dtT[:, t0:t0 + HT]), "c_dve", FENCE)
                        cv = P.val("c_dve")
                        for cc in range(8):
                            gc = h * 8 + cc
                            P.wait("tensor", "c_dve", max(cv, pstr_free))
                            P.step("tensor", lambda E, cc=cc: E.transpose(
                                ps_tr[:], a_tb[:, cc * 128:(cc + 1) * 128], k_I16[:]),
                                inc="c_pe")
                            pv = P.val("c_pe")
                            P.wait("vector", "c_pe", pv)
                            pstr_free = P.step("vector", lambda E, gc=gc: E.tensor_copy(
                                r_dtokb[:, gc, :], ps_tr[:]), inc="c_dve")
                        if h == 1:
                            cv = P.val("c_dve")
                            P.wait("sync", "c_dve", cv)
                            P.step("sync", lambda E: E.dma_start(
                                out=s_csr.ap().rearrange("c h l -> h c l"),
                                in_=r_csT.ap().rearrange("h (c l) -> h c l", l=128)),
                                inc="d_sp", n=16)
                            P.wait("tensor", "c_dve", cv)
                            P.step("tensor", lambda E: E.transpose(
                                ps_tr32[0:16, :], r_csT.ap().rearrange(
                                    "p (c l) -> p c l", l=128)[:, :, 127], k_I32[:]),
                                inc="c_pe")
                            pv = P.val("c_pe")
                            P.wait("vector", "c_pe", pv)
                            csv = P.step("vector", lambda E: E.tensor_copy(
                                a_csl[:], ps_tr32[0:16, :]), inc="c_dve")
                            P.wait("sync", "c_dve", csv)
                            dt_done["sp"] = P.step("sync", lambda E: E.dma_start(
                                out=s_cslast.ap(), in_=a_csl[:]), inc="d_csl", n=16)
                        continue

                    cw_m = m if m < M_DT else m - 1
                    P.wait("vector", "c_act", t1_free[par])
                    P.step("vector", lambda E, p=par, cm=cw_m: E.tensor_scalar_mul(
                        a_t1[p][:], a_row[p][:, 0:HT], k_cw[:, cm * 4:cm * 4 + 1]),
                        inc="c_dve")
                    for k in range(1, 4):
                        P.step("vector", lambda E, p=par, cm=cw_m, k=k:
                               E.scalar_tensor_tensor(
                                   a_t1[p][:], a_row[p][:, k:k + HT],
                                   k_cw[:, cm * 4 + k:cm * 4 + k + 1],
                                   a_t1[p][:], ALU.mult, ALU.add), inc="c_dve")
                    row_free[par] = P.val("c_dve")
                    cvv = P.val("c_dve")
                    P.wait("scalar", "c_dve", cvv)
                    P.wait("scalar", f"d_br{par}", brow_free[par])
                    ca = P.step("scalar", lambda E, p=par, cm=cw_m: E.activation(
                        out=a_brow[p][:], in_=a_t1[p][:], func=AF.Silu,
                        bias=k_cb[:, cm:cm + 1]), inc="c_act")
                    t1_free[par] = ca
                    P.wait("sync", "c_act", ca)
                    if kind == "x":
                        dst = s_xc.ap()[m * 128:(m + 1) * 128, tok0:tok0 + HT]
                    elif kind == "B":
                        dst = s_bt.ap()[(m - M_B0) * 128:(m - M_B0 + 1) * 128,
                                        tok0:tok0 + HT]
                    else:
                        dst = s_ct.ap()[(m - M_C0) * 128:(m - M_C0 + 1) * 128,
                                        tok0:tok0 + HT]
                    brow_free[par] = P.step("sync", lambda E, d=dst, p=par: E.dma_start(
                        out=d, in_=a_brow[p][:]), inc=f"d_br{par}", n=16)
                    if kind == "B":
                        j = m - M_B0
                        for cc in range(8):
                            P.wait("tensor", "c_act", ca)
                            P.wait("tensor", "c_dve", pstr_free)
                            P.wait("tensor", "d_btk", t2_free)
                            P.step("tensor", lambda E, p=par, cc=cc: E.transpose(
                                ps_tr[:], a_brow[p][:, cc * 128:(cc + 1) * 128],
                                k_I16[:]), inc="c_pe")
                            pv = P.val("c_pe")
                            P.wait("vector", "c_pe", pv)
                            P.wait("vector", "d_btk", t2_free)
                            pstr_free = P.step("vector", lambda E, cc=cc: E.tensor_copy(
                                a_t2.ap().bitcast(BF16)[:, cc * 128:(cc + 1) * 128],
                                ps_tr[:]), inc="c_dve")
                        cv = P.val("c_dve")
                        P.wait("sync", "c_dve", cv)
                        t2_free = P.step("sync", lambda E, j=j, t0=tok0: E.dma_start(
                            out=s_btok.ap()[t0:t0 + HT, j * 128:(j + 1) * 128]
                            .rearrange("(c l) n -> l c n", l=128),
                            in_=a_t2.ap().bitcast(BF16)[:, 0:HT]
                            .rearrange("l (c n) -> l c n", n=128)), inc="d_btk", n=16)
                    if h == 1 and "M" in phases:
                        for mm_ in mini_sched.get(m, []):
                            emit_mini_chunk(mm_)

            while pc_i < len(precast):
                emit_precast_one()
            if "M" in phases:
                P.wait_all_dma("gpsimd")
                if "dbgI" in debug:
                    P.step("gpsimd", lambda E: E.dma_start(out=s_dbgI.ap(), in_=s_mc.ap()),
                           inc="d_pool", n=16)
                P.step("gpsimd", lambda E: E.collective_compute(
                    "AllGather", ALU.bypass,
                    replica_groups=[[0, 1], [2, 3], [4, 5], [6, 7]],
                    ins=[s_mc.ap().opt()], outs=[s_gath.ap().opt()]), inc="c_cc")

        a_end = {s: P.val(s) for s in SEMS}
        P.phase_barrier()

        stB = contextlib.ExitStack()
        if "B" in phases:
            r_carr = SB(stB, "r_carr", [128, NG * 1024])
            r_cbf = SB(stB, "r_cbf", [128, NG * 1024], BF16)
            r_nw = SB(stB, "r_nw", [128, INTER], BF16)
            r_dgD = SB(stB, "r_dgD", [128, NH * 128], BF16)
            b_x = [SB(stB, f"b_x{i}", [128, 1024], BF16) for i in range(3)]
            b_g = [SB(stB, f"b_g{i}", [128, 1024], BF16) for i in range(3)]
            b_bt = [SB(stB, f"b_bt{i}", [128, NG * 128], BF16) for i in range(2)]
            b_ct = [SB(stB, f"b_ct{i}", [128, NG * 128], BF16) for i in range(2)]
            b_bk = [SB(stB, f"b_bk{i}", [128, NG * 128], BF16) for i in range(2)]
            b_csb = [SB(stB, f"b_csb{i}", [128, GH * 128]) for i in range(2)]
            b_cst = SB(stB, "b_cst", [128, 128])
            b_ncs = SB(stB, "b_ncs", [128, 128])
            b_ecs = SB(stB, "b_ecs", [128, 128])
            b_bc = SB(stB, "b_bc", [128, 128])
            b_ecd = SB(stB, "b_ecd", [128, 128])
            b_dcy = SB(stB, "b_dcy", [128, 128])
            b_ddb = SB(stB, "b_ddb", [128, 128], BF16)
            b_seg = SB(stB, "b_seg", [128, GH * 128])
            b_L = SB(stB, "b_L", [128, GH * 128], BF16)
            b_sm = SB(stB, "b_sm", [128, 128], BF16)
            b_sLT = SB(stB, "b_sLT", [128, GH * 128], BF16)
            b_xdt = SB(stB, "b_xdt", [128, 1024], BF16)
            b_xdd = SB(stB, "b_xdd", [128, 1024], BF16)
            b_yos = SB(stB, "b_yos", [128, 1024])
            b_y = SB(stB, "b_y", [128, 1024])
            b_t = SB(stB, "b_t", [128, 1024])
            b_sg = SB(stB, "b_sg", [128, 1024])
            b_yg = SB(stB, "b_yg", [128, 1024])
            b_jk = SB(stB, "b_jk", [128, 1024])
            b_ssq = SB(stB, "b_ssq", [128, 4])
            b_nrm = [SB(stB, f"b_nrm{i}", [128, 1024], BF16) for i in range(2)]

            ps_s = PS(stB, "ps_s", [128, 128])
            ps_yd = PS(stB, "ps_yd", [128, 1024])
            ps_yo = PS(stB, "ps_yo", [128, 1024])
            ps_sst = PS(stB, "ps_sst", [128, 1024])

            nwv = P.step("gpsimd", lambda E: E.dma_start(out=r_nw[:], in_=i_nw),
                         inc="d_nw", n=16)
            P.wait("vector", "d_nw", nwv)
            for hh in range(NH):
                P.step("vector", lambda E, hh=hh: E.tensor_scalar_mul(
                    r_dgD[:, hh * 128:(hh + 1) * 128], k_I16[:], k_D[:, hh:hh + 1]),
                    inc="c_dve")
            P.wait("sync", "c_cc", P.val("c_cc"))
            if "dbgJ" in debug:
                P.step("sync", lambda E: E.dma_start(out=s_dbgJ.ap(), in_=s_gath.ap()),
                       inc="d_sp", n=16)
            impv = 0
            for g in range(NG):
                impv = P.step("sync", lambda E, g=g: E.dma_start(
                    out=r_carr[:, g * 1024:(g + 1) * 1024],
                    in_=s_gath.ap()[0:128, g * 1024:(g + 1) * 1024]), inc="d_imp", n=16)
            P.wait("vector", "d_imp", impv)
            P.step("vector", lambda E: E.tensor_scalar_mul(r_carr[:], r_carr[:], k_w0[:]),
                   inc="c_dve")
            cpv = P.step_fenced("vector", lambda E: E.tensor_copy(r_cbf[:], r_carr[:]),
                                "c_dve", FENCE)
            cbf_v = [P.val("c_dve")] * NG
            if "dbg1" in debug:
                P.wait("sync", "c_dve", cpv)
                P.step("sync", lambda E: E.dma_start(out=s_dbg1.ap(), in_=r_carr[:]),
                       inc="d_sp", n=16)

            nrm_free = [0, 0]
            war_x = [0, 0, 0]
            war_g = [0, 0, 0]
            war_bt = [0, 0]
            war_csb = [0, 0]
            ymv = 0
            yg_prev = 0
            nv_prev = 0
            sLT_used = 0
            xddv = 0
            seg_act = 0
            sltv = 0
            chunk_ca = 0

            b_count = 0
            NIT = NCH * NG
            ldx_it = [0] * NIT
            ldg_it = [0] * NIT

            def b_prefetch(j):
                if j >= NIT:
                    return
                cj, gj = j // NG, j % NG
                i = j % 3
                P.wait("scalar", "c_pe", war_x[i])
                ldx_it[j] = P.step("scalar", lambda E, c=cj, g=gj, i=i:
                                   E.dma_start_transpose(
                                       b_x[i][:], s_xc.ap()[g * 1024:(g + 1) * 1024,
                                                            c * 128:(c + 1) * 128]),
                                   inc=f"d_bx{i}", n=16)
                P.wait("scalar", "c_dve", war_g[i])
                ldg_it[j] = P.step("scalar", lambda E, c=cj, g=gj, i=i:
                                   E.dma_start_transpose(
                                       b_g[i][:], s_gate.ap()[g * 1024:(g + 1) * 1024,
                                                              c * 128:(c + 1) * 128]),
                                   inc=f"d_bg{i}", n=16)

            b_prefetch(0)
            b_prefetch(1)
            b_prefetch(2)
            for c in range(NCH):
                if b_count >= b_limit:
                    break
                p2 = c % 2
                P.wait("sync", "c_pe", war_bt[p2])
                ldbt = P.step("sync", lambda E, c=c, p=p2: E.dma_start(
                    out=b_bt[p].ap().rearrange("n (g s) -> n g s", g=NG),
                    in_=s_bt.ap().rearrange("(g n) t -> n g t", n=128)
                    [:, :, c * 128:(c + 1) * 128]), inc=f"d_bt{p2}", n=16)
                ldct = P.step("sync", lambda E, c=c, p=p2: E.dma_start(
                    out=b_ct[p].ap().rearrange("n (g s) -> n g s", g=NG),
                    in_=s_ct.ap().rearrange("(g n) t -> n g t", n=128)
                    [:, :, c * 128:(c + 1) * 128]), inc=f"d_bt{p2}", n=16)
                ldbk = P.step("sync", lambda E, c=c, p=p2: E.dma_start(
                    out=b_bk[p][:], in_=s_btok.ap()[c * 128:(c + 1) * 128, :]),
                    inc=f"d_bt{p2}", n=16)
                P.wait("sync", "c_act", chunk_ca)
                P.wait("sync", "c_dve", P.val("c_dve"))
                ldbc = P.step("sync", lambda E, c=c: E.dma_start(
                    out=b_bc[:], in_=s_cslast.ap()[c:c + 1, :].broadcast_to([128, 128])),
                    inc="d_bbc", n=16)
                P.wait("tensor", "c_dve", P.val("c_dve"))
                P.step("tensor", lambda E, c=c: E.transpose(
                    ps_tr32[:], r_csT[:, c * 128:(c + 1) * 128], k_I32[:]), inc="c_pe")
                pv = P.val("c_pe")
                P.wait("vector", "c_pe", pv)
                P.step("vector", lambda E: E.tensor_copy(b_cst[:], ps_tr32[:]), inc="c_dve")
                P.step("vector", lambda E: E.tensor_scalar_mul(b_ncs[:], b_cst[:], -1.0),
                       inc="c_dve")
                P.wait("vector", "d_bbc", ldbc)
                P.step("vector", lambda E: E.tensor_sub(b_dcy[:], b_bc[:], b_cst[:]),
                       inc="c_dve")
                cv = P.val("c_dve")
                P.wait("scalar", "c_dve", cv)
                P.step("scalar", lambda E: E.activation(out=b_ecs[:], in_=b_cst[:],
                                                        func=AF.Exp), inc="c_act")
                P.step("scalar", lambda E: E.activation(out=b_ecd[:], in_=b_bc[:],
                                                        func=AF.Exp), inc="c_act")
                ca = P.step("scalar", lambda E: E.activation(out=b_dcy[:], in_=b_dcy[:],
                                                             func=AF.Exp), inc="c_act")
                chunk_ca = ca
                P.wait("vector", "c_act", ca)
                P.wait("vector", "c_dve", xddv)
                P.step("vector", lambda E, c=c: E.tensor_tensor(
                    b_ddb[:], r_dtokb[:, c, :], b_dcy[:], ALU.mult), inc="c_dve")
                prep_v = P.val("c_dve")

                for g in range(NG):
                    b_count += 1
                    if b_count > b_limit:
                        break
                    j_it = c * NG + g
                    i = (c * NG + g) % 3
                    pcs = (c * NG + g) % 2
                    P.wait("sync", "c_dve", war_csb[pcs])
                    ldcsb = P.step("sync", lambda E, c=c, g=g, p=pcs: E.dma_start(
                        out=b_csb[p][:],
                        in_=s_csr.ap()[c, g * GH:(g + 1) * GH, :]
                        .rearrange("h l -> (h l)")[None, :]
                        .broadcast_to([128, GH * 128])), inc=f"d_cs{pcs}", n=16)
                    P.wait("tensor", f"d_bt{p2}", max(ldbt, ldct))
                    P.wait("tensor", "c_dve", P.val("c_dve"))
                    pv = P.step("tensor", lambda E, g=g, p=p2: E.matmul(
                        ps_s[:], b_bt[p][:, g * 128:(g + 1) * 128],
                        b_ct[p][:, g * 128:(g + 1) * 128], start=True, stop=True),
                        inc="c_pe")
                    P.wait("vector", "c_pe", pv)
                    P.wait("vector", "c_pe", sLT_used)
                    P.step("vector", lambda E: E.tensor_tensor(
                        b_sm[:], ps_s[:], k_tri[:], ALU.mult), inc="c_dve")
                    P.wait("vector", f"d_cs{pcs}", ldcsb)
                    P.wait("vector", "c_dve", prep_v)
                    P.wait("vector", "c_act", seg_act)
                    for hh in range(GH):
                        P.step("vector", lambda E, hh=hh, p=pcs, g=g: E.tensor_scalar(
                            b_seg[:, hh * 128:(hh + 1) * 128],
                            b_csb[p][:, hh * 128:(hh + 1) * 128],
                            b_ncs[:, g * GH + hh:g * GH + hh + 1], SEG_CLAMP,
                            ALU.add, ALU.min), inc="c_dve")
                    segv = P.val("c_dve")
                    war_csb[pcs] = segv
                    P.wait("scalar", "c_dve", segv)
                    P.wait("scalar", "c_dve", sltv)
                    ca = P.step("scalar", lambda E: E.activation(
                        out=b_L[:], in_=b_seg[:], func=AF.Exp), inc="c_act")
                    seg_act = ca
                    P.wait("vector", "c_act", ca)
                    sltv = P.step_fenced("vector", lambda E: E.tensor_tensor(
                        b_sLT.ap().rearrange("p (h l) -> p h l", l=128),
                        b_sm[:, None, :].broadcast_to([128, GH, 128]),
                        b_L.ap().rearrange("p (h l) -> p h l", l=128), ALU.mult),
                        "c_dve", FENCE)
                    P.wait("vector", f"d_bx{i}", ldx_it[j_it])
                    P.wait("vector", "c_pe", ymv)
                    xdtv = P.step_fenced("vector", lambda E, g=g, i=i, c=c: E.tensor_tensor(
                        b_xdt.ap().rearrange("p (h e) -> p h e", e=HD),
                        b_x[i].ap().rearrange("p (h e) -> p h e", e=HD),
                        r_dtokb[:, c, g * GH:(g + 1) * GH][:, :, None]
                        .broadcast_to([128, GH, HD]), ALU.mult), "c_dve", FENCE)
                    xddv = P.step_fenced("vector", lambda E, g=g, i=i: E.tensor_tensor(
                        b_xdd.ap().rearrange("p (h e) -> p h e", e=HD),
                        b_x[i].ap().rearrange("p (h e) -> p h e", e=HD),
                        b_ddb[:, g * GH:(g + 1) * GH, None].broadcast_to([128, GH, HD]),
                        ALU.mult), "c_dve", FENCE)
                    P.wait("tensor", "c_dve", max(sltv, xdtv, xddv))
                    P.wait("tensor", "c_dve", cbf_v[g])
                    P.wait("tensor", f"d_bt{p2}", ldbk)

                    def ymms(E, g=g, i=i, p=p2):
                        for hh in range(GH):
                            E.matmul(ps_yd[:, hh * HD:(hh + 1) * HD],
                                     b_sLT[:, hh * 128:(hh + 1) * 128],
                                     b_xdt[:, hh * HD:(hh + 1) * HD],
                                     start=True, stop=False)
                            E.matmul(ps_yd[:, hh * HD:(hh + 1) * HD],
                                     r_dgD[:, (g * GH + hh) * 128:(g * GH + hh + 1) * 128],
                                     b_x[i][:, hh * HD:(hh + 1) * HD],
                                     start=False, stop=True)
                        for hh in range(GH):
                            E.matmul(ps_yo[:, hh * HD:(hh + 1) * HD],
                                     b_ct[p][:, g * 128:(g + 1) * 128],
                                     r_cbf[:, (g * GH + hh) * HD:(g * GH + hh + 1) * HD],
                                     start=True, stop=True)
                        for hh in range(GH):
                            ins = E.matmul(ps_sst[:, hh * HD:(hh + 1) * HD],
                                           b_bk[p][:, g * 128:(g + 1) * 128],
                                           b_xdd[:, hh * HD:(hh + 1) * HD],
                                           start=True, stop=True)
                        return ins
                    ymv = P.step("tensor", ymms, inc="c_pe")
                    if c == 0 and g == 0 and "dbgC" in debug:
                        P.wait("sync", "c_dve", prep_v)
                        P.step("sync", lambda E: E.dma_start(
                            out=s_dbgC.ap(), in_=r_dtokb.ap().rearrange("p a b -> p (a b)")), inc="d_sp", n=16)
                        P.step("sync", lambda E: E.dma_start(out=s_dbgD.ap(), in_=b_ddb[:]), inc="d_sp", n=16)
                    if c == 0 and g == 0 and "dbgE" in debug:
                        P.wait("sync", f"d_bx{i}", ldx_it[j_it])
                        P.step("sync", lambda E, i=i: E.dma_start(out=s_dbgE.ap(), in_=b_x[i][:]), inc="d_sp", n=16)
                    if c == 0 and g == 0 and "dbg2" in debug:
                        P.wait("sync", "c_dve", max(sltv, xdtv))
                        P.step("sync", lambda E: E.dma_start(out=s_dbg2.ap(), in_=b_seg[:]), inc="d_sp", n=16)
                        P.step("sync", lambda E: E.dma_start(out=s_dbg3.ap(), in_=b_L[:]), inc="d_sp", n=16)
                        P.step("sync", lambda E: E.dma_start(out=s_dbg4.ap(), in_=b_sm[:]), inc="d_sp", n=16)
                        P.step("sync", lambda E: E.dma_start(out=s_dbg5.ap(), in_=b_xdt[:]), inc="d_sp", n=16)
                        P.step("sync", lambda E, p=pcs: E.dma_start(out=s_dbg7.ap(), in_=b_csb[p][:]), inc="d_sp", n=16)
                        P.step("sync", lambda E: E.dma_start(out=s_dbgF.ap(), in_=b_sLT[:]), inc="d_sp", n=16)
                    war_x[i] = ymv
                    war_bt[p2] = ymv
                    sLT_used = ymv
                    if c == 0 and g == 0 and "dbgG" in debug:
                        P.wait("vector", "c_pe", ymv)
                        dg1 = P.step("vector", lambda E: E.tensor_copy(b_jk[:], ps_yd[:]), inc="c_dve")
                        P.wait("sync", "c_dve", dg1)
                        P.step("sync", lambda E: E.dma_start(out=s_dbgG.ap(), in_=b_jk[:]), inc="d_sp", n=16)
                        dg2 = P.step("vector", lambda E: E.tensor_copy(b_jk[:], ps_yo[:]), inc="c_dve")
                        P.wait("sync", "c_dve", dg2)
                        dg3 = P.step("sync", lambda E: E.dma_start(out=s_dbgH.ap(), in_=b_jk[:]), inc="d_sp", n=16)
                        P.wait("vector", "d_sp", dg3)
                    P.wait("vector", "c_pe", ymv)
                    
                    P.step("vector", lambda E, g=g: E.tensor_tensor(
                        b_yos.ap().rearrange("p (h e) -> p h e", e=HD),
                        ps_yo.ap().rearrange("p (h e) -> p h e", e=HD),
                        b_ecs[:, g * GH:(g + 1) * GH, None].broadcast_to([128, GH, HD]),
                        ALU.mult), inc="c_dve")
                    yv = P.step("vector", lambda E: E.tensor_tensor(
                        b_y[:], b_yos[:], ps_yd[:], ALU.add), inc="c_dve")
                    P.step("vector", lambda E, g=g: E.tensor_tensor(
                        b_t.ap().rearrange("p (h e) -> p h e", e=HD),
                        r_carr.ap().rearrange("p (g h e) -> p g h e", g=NG, e=HD)[:, g],
                        b_ecd[:, g * GH:(g + 1) * GH, None].broadcast_to([128, GH, HD]),
                        ALU.mult), inc="c_dve")
                    scv = P.step("vector", lambda E, g=g: E.tensor_tensor(
                        r_carr[:, g * 1024:(g + 1) * 1024], b_t[:], ps_sst[:], ALU.add),
                        inc="c_dve")
                    if c == 0 and g == 0 and "dbg6" in debug:
                        P.wait("sync", "c_dve", yv)
                        P.step("sync", lambda E: E.dma_start(out=s_dbg6.ap(), in_=b_y[:]), inc="d_sp", n=16)
                    cbf_v[g] = P.step_fenced("vector", lambda E, g=g: E.tensor_copy(
                        r_cbf[:, g * 1024:(g + 1) * 1024],
                        r_carr[:, g * 1024:(g + 1) * 1024]), "c_dve", FENCE)
                    P.wait("scalar", f"d_bg{i}", ldg_it[j_it])
                    P.wait("scalar", "c_dve", max(nv_prev, yg_prev))
                    sga = P.step("scalar", lambda E, i=i: E.activation(
                        out=b_sg[:], in_=b_g[i][:], func=AF.Silu), inc="c_act")
                    P.wait("vector", "c_act", sga)
                    yg_prev = P.step("vector", lambda E: E.tensor_tensor(
                        b_yg[:], b_y[:], b_sg[:], ALU.mult), inc="c_dve")
                    war_g[i] = yg_prev
                    P.wait("scalar", "c_dve", yg_prev)
                    P.step("scalar", lambda E: E.activation(
                        out=b_jk[:], in_=b_yg[:], func=AF.Square,
                        accum_out=b_ssq[:, 0:1]), inc="c_act")
                    P.step("scalar", lambda E: E.activation(
                        out=b_ssq[:, 1:2], in_=b_ssq[:, 0:1], func=AF.Ln,
                        scale=k_cv[:, 2:3], bias=k_cv[:, 0:1]), inc="c_act")
                    rsv = P.step("scalar", lambda E: E.activation(
                        out=b_ssq[:, 0:1], in_=b_ssq[:, 1:2], func=AF.Exp, scale=k_cv[:, 1:2]),
                        inc="c_act")
                    np_ = (c * NG + g) % 2
                    if c == 0 and g == 0 and "dbg8" in debug:
                        P.wait("sync", "c_act", rsv)
                        P.step("sync", lambda E: E.dma_start(out=s_dbg8.ap(), in_=b_sg[:]), inc="d_sp", n=16)
                        P.step("sync", lambda E: E.dma_start(out=s_dbg9.ap(), in_=b_yg[:]), inc="d_sp", n=16)
                        P.step("sync", lambda E: E.dma_start(out=s_dbgA.ap(), in_=b_ssq[:]), inc="d_sp", n=16)
                        P.step("sync", lambda E: E.dma_start(out=s_dbgB.ap(), in_=r_nw[:]), inc="d_sp", n=16)
                    P.wait("vector", "c_act", rsv)
                    P.wait("vector", "d_nw", nwv)
                    P.wait("vector", f"d_nrm{np_}", nrm_free[np_])
                    nv_prev = P.step("vector", lambda E, g=g, np_=np_:
                                     E.scalar_tensor_tensor(
                                         b_nrm[np_][:], b_yg[:], b_ssq[:, 0:1],
                                         r_nw[:, g * 1024:(g + 1) * 1024],
                                         ALU.mult, ALU.mult), inc="c_dve")
                    P.wait("sync", "c_dve", nv_prev)
                    nrm_free[np_] = P.step("sync", lambda E, c=c, g=g, np_=np_:
                                           E.dma_start(
                                               out=s_y.ap()[c * 128:(c + 1) * 128,
                                                            g * 1024:(g + 1) * 1024],
                                               in_=b_nrm[np_][:]), inc=f"d_nrm{np_}", n=16)
                    b_prefetch(j_it + 3)

        stB.close()
        b_end = {s: P.val(s) for s in SEMS}
        stAB.close()
        P.phase_barrier()

        stC = contextlib.ExitStack()
        if "C" in phases:
            c_yT = SB(stC, "c_yT", [128, 64, 512], BF16)
            c_wv = [SB(stC, f"c_wv{i}", [128, 64, 256], BF16) for i in range(2)]
            c_ab = SB(stC, "c_ab", [128, 4, D], BF16)
            c_xin = SB(stC, "c_xin", [128, D])
            c_x1 = SB(stC, "c_x1", [128, D])
            c_jk = SB(stC, "c_jk", [128, 1024])
            c_sq = SB(stC, "c_sq", [128, 4])
            c_st = SB(stC, "c_st", [128, 4])
            c_x2 = SB(stC, "c_x2", [128, D], BF16)
            ps_c = PS(stC, "ps_c", [128, 1024])

            x2_store = {}
            pe_mbdone = 0
            qa_prev = 0
            x2st_prev = 0
            xin_free = 0
            for Mb in range(4):
                ytl = 0
                P.wait("scalar", "c_pe", pe_mbdone)
                for kt in range(64):
                    ytl = P.step("scalar", lambda E, Mb=Mb, kt=kt: E.dma_start_transpose(
                        c_yT[:, kt, :], s_y.ap()[Mb * 512:(Mb + 1) * 512,
                                                 kt * 128:(kt + 1) * 128]),
                        inc="d_yt", n=16)
                wvl = [0, 0]
                evac_v = [0, 0]

                def wave_dma_c(w):
                    P.wait("scalar", "d_wbs0", P.val("d_wbs0"))
                    P.wait("scalar", "d_wbs1", P.val("d_wbs1"))
                    wvl[w % 2] = P.step("scalar", lambda E, w=w: E.dma_start(
                        out=c_wv[w % 2].ap().rearrange("p a b -> p (a b)"),
                        in_=s_wopb[w].rearrange("p a b -> p (a b)")), inc=f"d_cw{w % 2}", n=16)

                wave_dma_c(0)
                wave_dma_c(1)
                for w in range(16):
                    P.wait("tensor", f"d_cw{w % 2}", wvl[w % 2])
                    P.wait("tensor", "d_yt", ytl)
                    for m in range(4):
                        pp = m % 2
                        P.wait("tensor", "c_dve", evac_v[pp])

                        def cmm(E, w=w, m=m, pp=pp):
                            for k in range(64):
                                ins = E.matmul(ps_c[:, pp * 512:pp * 512 + 256],
                                               c_yT[:, k, m * 128:(m + 1) * 128],
                                               c_wv[w % 2][:, k, :],
                                               start=(k == 0), stop=(k == 63))
                            return ins
                        pv = P.step("tensor", cmm, inc="c_pe")
                        P.wait("vector", "c_pe", pv)
                        evac_v[pp] = P.step("vector", lambda E, w=w, m=m, pp=pp:
                                            E.tensor_copy(
                                                c_ab[:, m, w * 256:(w + 1) * 256],
                                                ps_c[:, pp * 512:pp * 512 + 256]),
                                            inc="c_dve")
                    if w + 2 < 16:
                        P.wait("scalar", "c_pe", P.val("c_pe"))
                        wave_dma_c(w + 2)
                pe_mbdone = P.val("c_pe")
                for m in range(4):
                    mi = Mb * 4 + m
                    P.wait("sync", "c_dve", xin_free)
                    xl = P.step("sync", lambda E, mi=mi: E.dma_start(
                        out=c_xin[:], in_=i_xres[mi * 128:(mi + 1) * 128, :]),
                        inc="d_xin", n=16)
                    P.wait("vector", "d_xin", xl)
                    P.wait("vector", "c_dve", evac_v[m % 2])
                    P.wait("vector", "c_act", qa_prev)
                    xin_free = P.step("vector", lambda E, m=m: E.tensor_add(
                        c_x1[:], c_ab[:, m, :], c_xin[:]), inc="c_dve")
                    P.step("vector", lambda E: E.reduce_sum(c_st[:, 0:1], c_x1[:], AXX),
                           inc="c_dve")
                    sv = P.val("c_dve")
                    P.wait("scalar", "c_dve", sv)
                    for q in range(4):
                        qa_prev = P.step("scalar", lambda E, q=q: E.activation(
                            out=c_jk[:], in_=c_x1[:, q * 1024:(q + 1) * 1024],
                            func=AF.Square, accum_out=c_sq[:, q:q + 1]), inc="c_act")
                    P.wait("vector", "c_act", qa_prev)
                    P.step("vector", lambda E: E.reduce_sum(c_st[:, 1:2], c_sq[:], AXX),
                           inc="c_dve")
                    P.step("vector", lambda E: E.tensor_scalar_mul(
                        c_st[:, 0:1], c_st[:, 0:1], 1.0 / D), inc="c_dve")
                    P.step("vector", lambda E: E.tensor_tensor(
                        c_st[:, 2:3], c_st[:, 0:1], c_st[:, 0:1], ALU.mult), inc="c_dve")
                    P.step("vector", lambda E: E.tensor_scalar_mul(
                        c_st[:, 1:2], c_st[:, 1:2], 1.0 / D), inc="c_dve")
                    vv = P.step("vector", lambda E: E.tensor_sub(
                        c_st[:, 1:2], c_st[:, 1:2], c_st[:, 2:3]), inc="c_dve")
                    P.wait("scalar", "c_dve", vv)
                    P.step("scalar", lambda E: E.activation(
                        out=c_st[:, 2:3], in_=c_st[:, 1:2], func=AF.Ln, bias=k_cv[:, 0:1]),
                        inc="c_act")
                    ra = P.step("scalar", lambda E: E.activation(
                        out=c_st[:, 1:2], in_=c_st[:, 2:3], func=AF.Exp, scale=k_cv[:, 1:2]),
                        inc="c_act")
                    P.wait("vector", "c_act", ra)
                    P.step("vector", lambda E: E.tensor_tensor(
                        c_st[:, 3:4], c_st[:, 0:1], c_st[:, 1:2], ALU.mult), inc="c_dve")
                    P.step("vector", lambda E: E.tensor_scalar_mul(
                        c_st[:, 3:4], c_st[:, 3:4], -1.0), inc="c_dve")
                    P.wait("vector", "d_x2s", x2st_prev)
                    xv = P.step("vector", lambda E: E.tensor_scalar(
                        c_x2[:], c_x1[:], c_st[:, 1:2], c_st[:, 3:4], ALU.mult, ALU.add),
                        inc="c_dve")
                    P.wait("sync", "c_dve", xv)
                    x2st_prev = x2_store[mi] = P.step("sync", lambda E, mi=mi:
                                                      E.dma_start(
                                                          out=s_x2.ap()[mi * 128:(mi + 1) * 128, :],
                                                          in_=c_x2[:]), inc="d_x2s", n=16)

        stC.close()
        P.phase_barrier()

        stD = contextlib.ExitStack()
        if "D" in phases:
            d_xT = SB(stD, "d_xT", [128, 32, 512], BF16)
            d_wv = [SB(stD, f"d_wv{i}", [128, 32, 256], BF16) for i in range(2)]
            d_ac = SB(stD, "d_ac", [128, KT, 128], BF16)
            d_bf = SB(stD, "d_bf", [128, D], BF16)
            d_brow = SB(stD, "d_brow", [1, D], BF16)
            d_lw = SB(stD, "d_lw", [128, 128])
            d_h = SB(stD, "d_h", [128, 128])
            d_hb = SB(stD, "d_hb", [128, 128], BF16)
            d_hT = SB(stD, "d_hT", [128, 4, 128], BF16)
            d_ob = SB(stD, "d_ob", [128, 4, D], BF16)
            d_x2 = SB(stD, "d_x2", [128, D], BF16)
            d_x3 = SB(stD, "d_x3", [128, D])
            d_jk = SB(stD, "d_jk", [128, 1024])
            d_sq = SB(stD, "d_sq", [128, 4])
            d_st = SB(stD, "d_st", [128, 4])
            d_o = [SB(stD, f"d_o{i}", [128, D]) for i in range(2)]
            ps_d = PS(stD, "ps_d", [128, 1024])
            ps_hh = PS(stD, "ps_hh", [128, 128])

            P.step("gpsimd", lambda E: E.dma_start(
                out=d_ac.ap().rearrange("p a b -> p (a b)"),
                in_=i_acat.rearrange("p a b -> p (a b)")), inc="d_dcst", n=16)
            P.step("gpsimd", lambda E: E.dma_start(out=d_bf[:], in_=i_bflat),
                   inc="d_dcst", n=16)
            P.step("gpsimd", lambda E: E.dma_start(out=d_brow[:], in_=i_bias),
                   inc="d_dcst", n=16)
            dpl = P.step("gpsimd", lambda E: E.dma_start(out=d_lw[:], in_=i_lw),
                         inc="d_dcst", n=16)
            P.wait("vector", "d_dcst", dpl)

            pe_mbdone = 0
            qa_prev = 0
            ost = [0, 0]
            dx2_free = 0
            tr_prev = 0
            for Mb in range(4):
                xtl = 0
                P.wait("scalar", "c_pe", pe_mbdone)
                P.wait("scalar", "d_x2s", x2_store[Mb * 4 + 3])
                for kt in range(32):
                    xtl = P.step("scalar", lambda E, Mb=Mb, kt=kt: E.dma_start_transpose(
                        d_xT[:, kt, :], s_x2.ap()[Mb * 512:(Mb + 1) * 512,
                                                  kt * 128:(kt + 1) * 128]),
                        inc="d_dxt", n=16)
                hT_v = 0
                for m in range(4):
                    P.wait("tensor", "d_dxt", xtl)
                    P.wait("tensor", "d_dcst", dpl)
                    P.wait("tensor", "c_dve", hT_v)

                    def lmm(E, m=m):
                        for k in range(KT):
                            ins = E.matmul(ps_hh[:], d_xT[:, k, m * 128:(m + 1) * 128],
                                           d_ac[:, k, :], start=(k == 0),
                                           stop=(k == KT - 1))
                        return ins
                    pv = P.step("tensor", lmm, inc="c_pe")
                    P.wait("vector", "c_pe", pv)
                    P.step("vector", lambda E: E.scalar_tensor_tensor(
                        d_h[:], ps_hh[:], 2.0, d_lw[:], ALU.mult, ALU.mult), inc="c_dve")
                    P.wait("vector", "c_pe", tr_prev)
                    hv = P.step_fenced("vector", lambda E: E.tensor_copy(d_hb[:], d_h[:]),
                                       "c_dve", FENCE)
                    P.wait("tensor", "c_dve", hv)
                    tr_prev = P.step("tensor", lambda E: E.transpose(
                        ps_tr[:], d_hb[:], k_I16[:]), inc="c_pe")
                    pv = P.val("c_pe")
                    P.wait("vector", "c_pe", pv)
                    hT_v = P.step_fenced("vector", lambda E, m=m: E.tensor_copy(
                        d_hT[:, m, :], ps_tr[:]), "c_dve", FENCE)
                wvl = [0, 0]
                evac_v = [0, 0]

                def wave_dma_d(w):
                    wvl[w % 2] = P.step("scalar", lambda E, w=w: E.dma_start(
                        out=d_wv[w % 2].ap().rearrange("p a b -> p (a b)"),
                        in_=s_wffb[w].rearrange("p a b -> p (a b)")), inc=f"d_dw{w % 2}", n=16)

                wave_dma_d(0)
                wave_dma_d(1)
                for w in range(16):
                    P.wait("tensor", f"d_dw{w % 2}", wvl[w % 2])
                    P.wait("tensor", "c_dve", hT_v)
                    for m in range(4):
                        pp = m % 2
                        P.wait("tensor", "c_dve", evac_v[pp])

                        def dmm(E, w=w, m=m, pp=pp):
                            for k in range(KT):
                                E.matmul(ps_d[:, pp * 512:pp * 512 + 256],
                                         d_xT[:, k, m * 128:(m + 1) * 128],
                                         d_wv[w % 2][:, k, :],
                                         start=(k == 0), stop=False)
                            E.matmul(ps_d[:, pp * 512:pp * 512 + 256], k_ones[:],
                                     d_brow[:, w * 256:(w + 1) * 256],
                                     start=False, stop=False)
                            return E.matmul(ps_d[:, pp * 512:pp * 512 + 256],
                                            d_hT[:, m, :],
                                            d_bf[:, w * 256:(w + 1) * 256],
                                            start=False, stop=True)
                        pv = P.step("tensor", dmm, inc="c_pe")
                        P.wait("vector", "c_pe", pv)
                        evac_v[pp] = P.step("vector", lambda E, w=w, m=m, pp=pp:
                                            E.tensor_copy(
                                                d_ob[:, m, w * 256:(w + 1) * 256],
                                                ps_d[:, pp * 512:pp * 512 + 256]),
                                            inc="c_dve")
                    if w + 2 < 16:
                        P.wait("scalar", "c_pe", P.val("c_pe"))
                        wave_dma_d(w + 2)
                pe_mbdone = P.val("c_pe")
                for m in range(4):
                    mi = Mb * 4 + m
                    P.wait("sync", "c_dve", dx2_free)
                    xl = P.step("sync", lambda E, mi=mi: E.dma_start(
                        out=d_x2[:], in_=s_x2.ap()[mi * 128:(mi + 1) * 128, :]),
                        inc="d_dx2", n=16)
                    P.wait("vector", "d_dx2", xl)
                    P.wait("vector", "c_dve", evac_v[m % 2])
                    P.wait("vector", "c_act", qa_prev)
                    dx2_free = P.step("vector", lambda E, m=m: E.tensor_add(
                        d_x3[:], d_ob[:, m, :], d_x2[:]), inc="c_dve")
                    P.step("vector", lambda E: E.reduce_sum(d_st[:, 0:1], d_x3[:], AXX),
                           inc="c_dve")
                    sv = P.val("c_dve")
                    P.wait("scalar", "c_dve", sv)
                    for q in range(4):
                        qa_prev = P.step("scalar", lambda E, q=q: E.activation(
                            out=d_jk[:], in_=d_x3[:, q * 1024:(q + 1) * 1024],
                            func=AF.Square, accum_out=d_sq[:, q:q + 1]), inc="c_act")
                    P.wait("vector", "c_act", qa_prev)
                    P.step("vector", lambda E: E.reduce_sum(d_st[:, 1:2], d_sq[:], AXX),
                           inc="c_dve")
                    P.step("vector", lambda E: E.tensor_scalar_mul(
                        d_st[:, 0:1], d_st[:, 0:1], 1.0 / D), inc="c_dve")
                    P.step("vector", lambda E: E.tensor_tensor(
                        d_st[:, 2:3], d_st[:, 0:1], d_st[:, 0:1], ALU.mult), inc="c_dve")
                    P.step("vector", lambda E: E.tensor_scalar_mul(
                        d_st[:, 1:2], d_st[:, 1:2], 1.0 / D), inc="c_dve")
                    vv = P.step("vector", lambda E: E.tensor_sub(
                        d_st[:, 1:2], d_st[:, 1:2], d_st[:, 2:3]), inc="c_dve")
                    P.wait("scalar", "c_dve", vv)
                    P.step("scalar", lambda E: E.activation(
                        out=d_st[:, 2:3], in_=d_st[:, 1:2], func=AF.Ln, bias=k_cv[:, 0:1]),
                        inc="c_act")
                    ra = P.step("scalar", lambda E: E.activation(
                        out=d_st[:, 1:2], in_=d_st[:, 2:3], func=AF.Exp, scale=k_cv[:, 1:2]),
                        inc="c_act")
                    P.wait("vector", "c_act", ra)
                    P.step("vector", lambda E: E.tensor_tensor(
                        d_st[:, 3:4], d_st[:, 0:1], d_st[:, 1:2], ALU.mult), inc="c_dve")
                    P.step("vector", lambda E: E.tensor_scalar_mul(
                        d_st[:, 3:4], d_st[:, 3:4], -1.0), inc="c_dve")
                    P.wait("vector", f"d_out{m % 2}", ost[m % 2])
                    ov = P.step("vector", lambda E, m=m: E.tensor_scalar(
                        d_o[m % 2][:], d_x3[:], d_st[:, 1:2], d_st[:, 3:4],
                        ALU.mult, ALU.add), inc="c_dve")
                    P.wait("sync", "c_dve", ov)
                    ost[m % 2] = P.step("sync", lambda E, mi=mi, m=m: E.dma_start(
                        out=o_out[mi * 128:(mi + 1) * 128, :], in_=d_o[m % 2][:]),
                        inc=f"d_out{m % 2}", n=16)

        stD.close()
        P.wait_all_dma("sync")

        for sname in P.cnt:
            P.handles[sname] = big.enter_context(nc.semaphore(sname))
        with nc.Block() as block:
            block.sync(lambda E: [f(E) for f in P.ops["sync"]])
            block.scalar(lambda E: [f(E) for f in P.ops["scalar"]])
            block.tensor(lambda E: [f(E) for f in P.ops["tensor"]])
            block.vector(lambda E: [f(E) for f in P.ops["vector"]])
            block.gpsimd(lambda E: [f(E) for f in P.ops["gpsimd"]])

    return nc



def _prep_shared(inp):
    W = np.ascontiguousarray(inp["L0_attn_in_proj_w"])
    WT = W.T
    cols = np.concatenate([
        np.arange(8192, 16384), np.arange(16384, 17408), np.arange(18432, 18560),
        np.arange(17408, 18432), np.arange(0, 8192)])
    WT_ord = np.ascontiguousarray(WT[:, cols])
    Wt = np.ascontiguousarray(WT_ord.reshape(KT, 128, N_MT, 128).transpose(2, 1, 0, 3))

    cw = inp["L0_attn_conv_w"].reshape(10240, 4)
    cwT = np.ascontiguousarray(cw.reshape(80, 128, 4).transpose(1, 0, 2).reshape(128, 320))
    cbT = np.ascontiguousarray(inp["L0_attn_conv_b"].reshape(80, 128).T)
    dtb = np.ascontiguousarray(inp["L0_attn_dt_bias"].reshape(128, 1))
    Dtile = np.ascontiguousarray(np.broadcast_to(inp["L0_attn_D"][None, :], (128, 128)))
    normw = np.ascontiguousarray(np.broadcast_to(inp["L0_attn_norm_w"][None, :], (128, INTER)))
    WopT = np.ascontiguousarray(inp["L0_attn_out_proj_w"].T)
    WopT_t = np.ascontiguousarray(WopT.reshape(64, 128, 16, 256).transpose(2, 1, 0, 3))
    WffnT = np.ascontiguousarray(inp["L0_ffn_W"].T)
    WffnT_t = np.ascontiguousarray(WffnT.reshape(32, 128, 16, 256).transpose(2, 1, 0, 3))
    AcatT = np.ascontiguousarray(inp["L0_ffn_lora_A"].reshape(128, D).T)
    AcatT_t = np.ascontiguousarray(AcatT.reshape(KT, 128, 128).transpose(1, 0, 2))
    Bflat = np.ascontiguousarray(inp["L0_ffn_lora_B"].transpose(0, 2, 1).reshape(128, D))
    biasrow = np.ascontiguousarray(inp["L0_ffn_bias"].reshape(1, D))
    lw_rep = np.repeat(inp["L0_ffn_lora_weights"], 16)
    lwtile = np.ascontiguousarray(np.broadcast_to(lw_rep[None, :], (128, 128)))
    I128 = np.eye(128, dtype=np.float32)
    triu01 = np.triu(np.ones((128, 128), np.float32))
    Acol = -(np.arange(1, 129, dtype=np.float32)).reshape(128, 1)
    cvec = np.ascontiguousarray(np.broadcast_to(
        np.array([EPS, -0.5, 1.0 / 1024.0, 0.0], np.float32)[None, :], (128, 4)))
    return dict(cvec=cvec, Wt=Wt, cwT=cwT, cbT=cbT, dtb=dtb, Dtile=Dtile, normw=normw,
                WopT=WopT_t, WffnT=WffnT_t, AcatT=AcatT_t, Bflat=Bflat,
                biasrow=biasrow, lwtile=lwtile, I128=I128, triu01=triu01, Acol=Acol)


def make_in_maps(inp):
    shared = _prep_shared(inp)
    x = np.ascontiguousarray(inp["L0_attn_hidden_states"])
    in_maps = []
    for core in range(8):
        b, half = core // 2, core % 2
        xs = x[b, half * T:(half + 1) * T, :]
        if half == 0:
            halo = np.zeros((HALO, D), np.float32)
        else:
            halo = x[b, half * T - HALO:half * T, :]
        x_h = np.concatenate([halo, xs], axis=0)
        xT = np.ascontiguousarray(x_h.T)
        w0 = np.full((128, 1), 1.0 if half == 1 else 0.0, np.float32)
        in_maps.append(dict(
            xT=xT, xres=np.ascontiguousarray(xs),
            Wt=shared["Wt"], cwT=shared["cwT"], cbT=shared["cbT"], dtb=shared["dtb"],
            Dtile=shared["Dtile"], normw=shared["normw"], WopT=shared["WopT"],
            WffnT=shared["WffnT"], AcatT=shared["AcatT"], Bflat=shared["Bflat"],
            biasrow=shared["biasrow"], lwtile=shared["lwtile"], I128=shared["I128"],
            triu01=shared["triu01"], Acol=shared["Acol"], w0mask=w0,
            cvec=shared["cvec"]))
    return in_maps


def kernel(**inputs):
    in_maps = make_in_maps(inputs)
    nc = build_program()
    res = run_bass_kernel_spmd(nc, in_maps, core_ids=list(range(8)))
    out = np.empty((4, 4096, D), np.float32)
    for core in range(8):
        b, half = core // 2, core % 2
        out[b, half * T:(half + 1) * T, :] = res.results[core]["out"]
    return out
